# version 03
# baseline (speedup 1.0000x reference)
import numpy as np
import ml_dtypes
from contextlib import ExitStack

import concourse.bass as bass
import concourse.mybir as mybir
from concourse.bass_utils import run_bass_kernel_spmd

F32 = mybir.dt.float32
BF16 = mybir.dt.bfloat16
I32 = mybir.dt.int32
AF = mybir.ActivationFunctionType
ALU = mybir.AluOpType
AX = mybir.AxisListType
BF = ml_dtypes.bfloat16

D = 4096
NT = 1024
NC = 32
NH = 32
HD = 128
NEG = -1e30
SCALE = float(HD) ** -0.5
ORDER = [0, 16, 4, 20, 8, 24, 12, 28]

QW = {
    "qkv": dict(NGK=32, slots=4),
    "o":   dict(NGK=32, slots=4),
    "gu":  dict(NGK=32, slots=22),
    "dn":  dict(NGK=86, slots=4),
}
W1BUF = {"qkv": "a", "o": "a", "dn": "a", "gu": "g"}
W1OFF = {}
_offs = {"a": 0, "g": 0}
for _w in ("qkv", "o", "dn", "gu"):
    W1OFF[_w] = _offs[W1BUF[_w]]
    _offs[W1BUF[_w]] += QW[_w]["slots"] * 128 * QW[_w]["NGK"] * 128
W1A_PER_RANK = _offs["a"]
W1G_PER_RANK = _offs["g"]

DEBUG_TAPS = []
ALL_PHASES = ["init", "ln0", "qkv", "kvag", "ffn0", "deq", "attn", "wo",
              "ln1", "ffn1", "l1qkvo", "l1gudn", "fin"]
PHASES = list(ALL_PHASES)


def sl(t, a, b):
    return t[:, a:b]


class Ctx:
    ENGINES = ("sync", "gpsimd", "scalar", "vector", "tensor")

    def __init__(self, nc, es):
        self.nc = nc
        self.es = es
        self.streams = {e: [] for e in self.ENGINES}
        self.sems = {}
        self.counts = {}

    def sem(self, name):
        if name not in self.sems:
            self.sems[name] = self.es.enter_context(self.nc.semaphore(name))
            self.counts[name] = 0
        return self.sems[name]

    def inc(self, name, k=1):
        self.sem(name)
        self.counts[name] += k
        return self.counts[name]

    def tot(self, name):
        self.sem(name)
        return self.counts[name]

    def em(self, eng, fn):
        self.streams[eng].append(fn)

    def wait(self, eng, name, val=None):
        if val is None:
            val = self.tot(name)
        if val <= 0:
            return
        s = self.sem(name)
        self.em(eng, lambda e, s=s, v=val: e.wait_ge(s, v))

    def op(self, eng, sem_name, build):
        tgt = self.inc(sem_name)
        s = self.sems[sem_name]
        self.em(eng, lambda e, s=s, b=build: b(e).then_inc(s, 1))
        return tgt

    def dma(self, eng, sem_name, out, in_, **kw):
        tgt = self.inc(sem_name, 16)
        s = self.sems[sem_name]
        self.em(eng, lambda e, s=s, o=out, i=in_, kw=kw:
                e.dma_start(out=o, in_=i, **kw).then_inc(s, 16))
        return tgt

    def sbuf(self, name, shape, dtype):
        return self.es.enter_context(self.nc.sbuf_tensor(name, list(shape), dtype))

    def psum(self, name, shape, dtype=F32):
        return self.es.enter_context(self.nc.psum_tensor(name, list(shape), dtype))


def dap(t, offset, dims):
    return bass.AP(t.tensor, offset, dims)


def build_program(phases, taps=(), standalone=False):
    nc = bass.Bass("TRN2", target_bir_lowering=False)
    es = ExitStack()
    taps = set(taps)

    def ein(name, shape, dtype=F32):
        return nc.dram_tensor(name, list(shape), dtype, kind="ExternalInput").ap()

    def eout(name, shape, dtype=F32):
        return nc.dram_tensor(name, list(shape), dtype, kind="ExternalOutput").ap()

    def ibuf(name, shape, dtype=F32, shared=False):
        return nc.dram_tensor(name, list(shape), dtype, kind="Internal",
                              addr_space=("Shared" if shared else "Local")).ap()

    io = {}
    io["x0fm"] = ein("x0fm", [NC, 128, NT])
    for w in ("wq", "wk", "wo", "wb", "w1f"):
        io[w + "_st"] = ein(w + "_st", [32, 128, 32, 128])
    io["wv_mv"] = ein("wv_mv", [16, 32, 128, 256])
    io["wrT"] = ein("wrT", [D, 8])
    io["aflatT"] = ein("aflatT", [NC, 128, 128])
    io["bflat"] = ein("bflat", [128, D])
    io["bbase"] = ein("bbase", [128, 32])
    io["cosb"] = ein("cosb", [128, NT])
    io["sinb"] = ein("sinb", [128, NT])
    io["cosbq"] = ein("cosbq", [128, NT])
    io["sinbq"] = ein("sinbq", [128, NT])
    io["maskq"] = ein("maskq", [4, 128, 128])
    io["ident"] = ein("ident", [128, 128], BF16)
    io["identf"] = ein("identf", [128, 128])
    io["e8"] = ein("e8", [8, 128])
    io["qkvqw"] = ein("qkvqw", [32, 128, 64], I32)
    io["qkvqz"] = ein("qkvqz", [32, 64], I32)
    io["qkvsc"] = ein("qkvsc", [32, 512])
    io["oqw"] = ein("oqw", [32, 128, 64], I32)
    io["oqz"] = ein("oqz", [32, 64], I32)
    io["osc"] = ein("osc", [32, 512])
    io["guqw"] = ein("guqw", [32, 128, 352], I32)
    io["guqz"] = ein("guqz", [32, 352], I32)
    io["gusc"] = ein("gusc", [32, 2816])
    io["dnqw"] = ein("dnqw", [86, 128, 64], I32)
    io["dnqz"] = ein("dnqz", [86, 64], I32)
    io["dnsc"] = ein("dnsc", [86, 512])
    io["ln1c"] = ein("ln1c", [128, 32])
    io["ln2c"] = ein("ln2c", [128, 32])
    io["gw1T"] = ein("gw1T", [D, 32])
    io["gw2T"] = ein("gw2T", [32, 8])
    io["gb1r"] = ein("gb1r", [1, 32])
    io["gb2r"] = ein("gb2r", [1, 8])
    io["lngr"] = ein("lngr", [1, 32])
    io["lnbr"] = ein("lnbr", [1, 32])
    io["laT"] = ein("laT", [NC, 128, 128])
    io["lbflat"] = ein("lbflat", [128, D])
    if standalone:
        io["accin"] = ein("accin", [NC, 128, NT])

    acc = eout("acc", [NC, 128, NT])
    io["acc"] = acc

    io["kv_in"] = ibuf("kv_in", [8192, NT], BF16)
    io["kv_all"] = ibuf("kv_all", [4 * 8192, NT], BF16)
    io["q_d"] = ibuf("q_d", [D, NT], BF16)
    io["w1_in_a"] = ibuf("w1_in_a", [W1A_PER_RANK // 128, 128], BF16)
    io["w1_all_a"] = ibuf("w1_all_a", [8 * (W1A_PER_RANK // 128), 128], BF16,
                          shared=True)
    io["w1_in_g"] = ibuf("w1_in_g", [W1G_PER_RANK // 128, 128], BF16)
    io["w1_all_g"] = ibuf("w1_all_g", [8 * (W1G_PER_RANK // 128), 128], BF16,
                          shared=True)
    io["scz_d"] = ibuf("scz_d", [86, 6144])
    io["cnt_in"] = ibuf("cnt_in", [8, 2])
    io["cnt_out"] = ibuf("cnt_out", [8, 2])
    io["h1s_d"] = ibuf("h1s_d", [NC, 128, NT], BF16)
    io["hq_d"] = ibuf("hq_d", [NC, 128, NT], BF16)
    io["xn1_d"] = ibuf("xn1_d", [NC, 128, NT], BF16)
    io["stat_d"] = ibuf("stat_d", [4, 2048])

    tap_out = {}
    for t in taps:
        if t in ("xn0", "xn1", "attnO0", "hq1", "h1s"):
            tap_out[t] = eout("tap_" + t, [NC, 128, NT], BF16)
        elif t in ("q0", "k0"):
            tap_out[t] = eout("tap_" + t, [D, NT], BF16)
        elif t == "v0":
            tap_out[t] = eout("tap_" + t, [D, NT], BF16)
        elif t in ("w0", "w1route", "lg0"):
            tap_out[t] = eout("tap_" + t, [128, 64])
        elif t == "rt0":
            tap_out[t] = eout("tap_rt0", [128, 8 * 40])
        elif t == "deq":
            tap_out[t] = eout("tap_deq", [W1A_PER_RANK // 128, 128], BF16)
            tap_out["deq_g"] = eout("tap_deq_g", [W1G_PER_RANK // 128, 128], BF16)
        elif t == "counts":
            tap_out[t] = eout("tap_counts", [8, 2])
        elif t == "hid1":
            tap_out[t] = eout("tap_hid1", [86 * 128, NT], BF16)

    with es:
        c = Ctx(nc, es)
        c.io = io
        c.taps = taps
        c.tap_out = tap_out

        B = {}
        B["bigX"] = c.sbuf("bigX", [128, 86 * 512], BF16)
        B["slab"] = c.sbuf("slab", [128, 24576], BF16)
        B["scr"] = c.sbuf("scr", [128, 2048], F32)
        B["scrb"] = c.sbuf("scrb", [128, 2048], BF16)
        B["stat"] = c.sbuf("stat", [128, 2048], F32)
        B["maskf"] = c.sbuf("maskf", [128, 512], F32)
        B["consb"] = c.sbuf("consb", [128, 128], BF16)
        B["consf"] = c.sbuf("consf", [128, 128], F32)
        B["onesb"] = c.sbuf("onesb", [128, 1], BF16)
        B["onesf"] = c.sbuf("onesf", [1, 128], F32)
        B["rowf"] = c.sbuf("rowf", [1, 2048], F32)
        B["probT"] = c.sbuf("probT", [128, 1024], BF16)
        B["rstd2b"] = c.sbuf("rstd2b", [128, NT], F32)
        B["sml"] = c.sbuf("sml", [128, 256], F32)
        B["smlb"] = c.sbuf("smlb", [128, 64], BF16)
        B["sczb"] = c.sbuf("sczb", [128, 1024], F32)
        B["pA"] = c.psum("pA", [128, 1024])
        B["pB"] = c.psum("pB", [128, 1024])
        B["pC"] = c.psum("pC", [128, 1024])
        B["pD"] = c.psum("pD", [128, 1024])

        if "init" in phases or standalone:
            for s2 in range(4):
                c.dma("sync", "lds", sl(B["maskf"], 128 * s2, 128 * (s2 + 1)),
                      io["maskq"][s2])
            c.dma("sync", "lds", B["consb"][:, 0:128], io["ident"])
            c.dma("sync", "lds", B["consf"][:, 0:128], io["identf"])
            c.op("gpsimd", "gp", lambda e: e.memset(B["onesb"][:], 1.0))
            c.op("gpsimd", "gp", lambda e: e.memset(B["onesf"][:], 1.0))
            src0 = io["accin"] if standalone else io["x0fm"]
            for i in range(8):
                c.dma("gpsimd", "acg", acc[4 * i:4 * (i + 1)],
                      src0[4 * i:4 * (i + 1)])

        if "ln0" in phases:
            src = io["accin"] if standalone else io["x0fm"]
            ph_ln(c, B, src, acc_add=False, eps=1e-5, name="ln0")
            if "xn0" in taps:
                tap_fm(c, tap_out["xn0"], B["bigX"])
        if "qkv" in phases:
            ph_qkv(c, B)
            if "q0" in taps:
                ph_tap_dram(c, tap_out["q0"], io["q_d"], [D, NT])
            if "k0" in taps:
                ph_tap_dram(c, tap_out["k0"], io["kv_in"][0:4096, :], [4096, NT])
            if "v0" in taps:
                ph_tap_dram(c, tap_out["v0"], io["kv_in"][4096:8192, :], [4096, NT])
        if "kvag" in phases:
            c.wait("gpsimd", "sts")
            ccs = c.sem("cc")
            c.inc("cc")
            c.em("gpsimd", lambda e, s=ccs: e.collective_compute(
                "AllGather", ALU.bypass,
                replica_groups=[[0, 1, 2, 3], [4, 5, 6, 7]],
                ins=[io["kv_in"]], outs=[io["kv_all"]]).then_inc(s, 1))
        if "ffn0" in phases:
            ph_ffn0(c, B)
        if "deq" in phases:
            ph_dequant(c, B)
            c.wait("gpsimd", "sts")
            ccs = c.sem("cc")
            for nm in ("a", "g"):
                c.inc("cc")
                c.em("gpsimd", lambda e, s=ccs, nm=nm: e.collective_compute(
                    "AllGather", ALU.bypass,
                    replica_groups=[[0, 1, 2, 3, 4, 5, 6, 7]],
                    ins=[io["w1_in_" + nm]],
                    outs=[io["w1_all_" + nm]]).then_inc(s, 1))
            if "deq" in taps:
                c.wait("sync", "sts")
                for nm, n in (("a", W1A_PER_RANK // 128),):
                    step = n // 8
                    for i in range(0, n, step):
                        c.dma("sync", "sts", tap_out["deq"][i:i + step],
                              io["w1_in_a"][i:i + step])
                ng = W1G_PER_RANK // 128
                step = ng // 8
                for i in range(0, ng, step):
                    c.dma("sync", "sts", tap_out["deq_g"][i:i + step],
                          io["w1_in_g"][i:i + step])
        if "attn" in phases:
            ph_attn(c, B)
            if "attnO0" in taps:
                tap_fm(c, tap_out["attnO0"], B["bigX"])
        if "wo" in phases:
            ph_proj_acc(c, B, io["wo_st"], sl(B["bigX"], 0, NC * NT), "wo")
        if "ln1" in phases:
            ph_ln(c, B, acc, acc_add=True, eps=1e-5, name="ln1")
            if "xn1" in taps:
                tap_fm(c, tap_out["xn1"], B["bigX"])
        if "ffn1" in phases:
            ph_ffn1(c, B)
        if "l1qkvo" in phases:
            ph_l1qkvo(c, B)
            if "hq1" in taps:
                ph_tap_dram(c, tap_out["hq1"].rearrange("a p t -> (a p) t"), io["hq_d"].rearrange("a p t -> (a p) t"), [D, NT])
            if "h1s" in taps:
                ph_tap_dram(c, tap_out["h1s"].rearrange("a p t -> (a p) t"), io["h1s_d"].rearrange("a p t -> (a p) t"), [D, NT])
        if "l1gudn" in phases:
            ph_l1gudn(c, B)
        if "fin" in phases:
            for s in ("acg", "sts", "lds", "ldg", "cc"):
                c.wait("sync", s)

        with nc.Block() as blk:
            for eng in Ctx.ENGINES:
                def run(e, eng=eng):
                    for fn in c.streams[eng]:
                        fn(e)
                getattr(blk, eng)(run)

    return nc


def tap_fm(c, out_t, buf, chunks=NC):
    for e in ("dv", "pe", "sc"):
        c.wait("sync", e)
    for dc in range(chunks):
        c.dma("sync", "sts", out_t[dc], sl(buf, NT * dc, NT * (dc + 1)))


def ph_tap_dram(c, out_t, src_ap, shape):
    for e in ("dv", "pe", "sc", "sts", "ldg"):
        c.wait("sync", e)
    n = shape[0]
    step = max(1, n // 8)
    for i in range(0, n, step):
        c.dma("sync", "sts", out_t[i:i + step], src_ap[i:i + step])


def ph_ln(c, B, src, acc_add, eps, name):
    bigA, scr, scrb, stat = sl(B["bigX"], 0, NC * NT), B["scr"], B["scrb"], B["stat"]
    rowf, onesb, onesf = B["rowf"], B["onesb"], B["onesf"]
    acc = c.io["acc"]
    if name == "ln1":
        c.wait("sync", "acg")
    sum_ps = B["pA"][0:1, :]
    sq_ps = B["pB"][0:1, :]
    for dc in range(NC):
        x_f = sl(scr, 0, NT)
        x_b = sl(scrb, 0, NT)
        sq_b = sl(scrb, NT, 2 * NT)
        c.wait("sync", "pe")
        c.wait("sync", "dv")
        c.dma("sync", "lds", x_f, src[dc])
        c.wait("vector", "lds")
        c.wait("vector", "pe")
        c.op("vector", "dv", lambda e, o=x_b, i=x_f: e.tensor_copy(o, i))
        c.wait("scalar", "dv")
        c.wait("scalar", "pe")
        c.op("scalar", "sc", lambda e, o=sq_b, i=x_b: e.activation(o, i, AF.Square))
        c.wait("tensor", "dv")
        c.wait("tensor", "sc")
        for p in range(2):
            c.op("tensor", "pe", lambda e, o=sum_ps[0:1, 512 * p:512 * (p + 1)],
                 r=sl(scrb, 512 * p, 512 * (p + 1)), s=(dc == 0),
                 t=(dc == NC - 1): e.matmul(o, onesb[:], r, start=s, stop=t))
            c.op("tensor", "pe", lambda e, o=sq_ps[0:1, 512 * p:512 * (p + 1)],
                 r=sl(scrb, NT + 512 * p, NT + 512 * (p + 1)), s=(dc == 0),
                 t=(dc == NC - 1): e.matmul(o, onesb[:], r, start=s, stop=t))
    nm = rowf[0:1, 0:NT]
    rstd = rowf[0:1, NT:2 * NT]
    msq = scr[0:1, 0:NT]
    c.wait("vector", "pe")
    c.op("vector", "dv", lambda e: e.tensor_scalar(
        out=nm, in0=sum_ps, scalar1=-1.0 / D, scalar2=None, op0=ALU.mult))
    c.op("vector", "dv", lambda e: e.tensor_scalar(
        out=rstd, in0=sq_ps, scalar1=1.0 / D, scalar2=None, op0=ALU.mult))
    c.wait("vector", "dv")
    c.op("vector", "dv", lambda e: e.tensor_mul(msq, nm, nm))
    c.wait("vector", "dv")
    c.op("vector", "dv", lambda e: e.tensor_sub(rstd, rstd, msq))
    c.wait("vector", "dv")
    c.op("vector", "dv", lambda e: e.tensor_scalar(
        out=rstd, in0=rstd, scalar1=float(eps), scalar2=None, op0=ALU.add))
    c.wait("scalar", "dv")
    c.op("scalar", "sc", lambda e: e.activation(rstd, rstd, AF.Sqrt))
    c.wait("vector", "sc")
    c.op("vector", "dv", lambda e: e.reciprocal(rstd, rstd))
    c.wait("tensor", "dv")
    for p in range(2):
        c.op("tensor", "pe", lambda e, o=B["pC"][:, 512 * p:512 * (p + 1)],
             r=rowf[0:1, 512 * p:512 * (p + 1)]:
             e.matmul(o, onesf[:], r, start=True, stop=True))
        c.op("tensor", "pe", lambda e, o=B["pD"][:, 512 * p:512 * (p + 1)],
             r=rowf[0:1, NT + 512 * p:NT + 512 * (p + 1)]:
             e.matmul(o, onesf[:], r, start=True, stop=True))
    c.wait("vector", "pe")
    c.op("vector", "dv", lambda e: e.tensor_copy(sl(stat, 0, NT), B["pC"][:]))
    c.op("vector", "dv", lambda e: e.tensor_copy(sl(stat, NT, 2 * NT), B["pD"][:]))
    nm_b = sl(stat, 0, NT)
    rstd_b = sl(stat, NT, 2 * NT)
    for dc in range(NC):
        x_f = sl(scr, 0, NT)
        xn_f = sl(scr, NT, 2 * NT)
        c.wait("sync", "dv")
        c.wait("sync", "acg")
        c.dma("sync", "lds", x_f, src[dc])
        c.wait("vector", "lds")
        c.wait("vector", "acg")
        c.op("vector", "dv", lambda e, o=xn_f, a=x_f: e.tensor_add(o, a, nm_b))
        c.wait("vector", "dv")
        dstc = bigA[:, NT * dc:NT * (dc + 1)]
        c.op("vector", "dv", lambda e, o=dstc, a=xn_f: e.tensor_mul(o, a, rstd_b))
        if acc_add:
            c.op("vector", "dv", lambda e, a=xn_f: e.tensor_mul(a, a, rstd_b))
            c.wait("gpsimd", "dv")
            c.dma("gpsimd", "acg", acc[dc], xn_f, accum_op=ALU.add)
            c.wait("sync", "dv")
            c.dma("sync", "sts", c.io["xn1_d"][dc], dstc)


def ph_qkv(c, B):
    bigA, slab, scr, scrb = (sl(B["bigX"], 0, NC * NT), B["slab"], B["scr"],
                             B["scrb"])
    cons = B["stat"]
    io = c.io
    qe_ps, qo_ps = B["pA"], B["pB"]

    for wname, dst in (("wq_st", io["q_d"]), ("wk_st", io["kv_in"])):
        c.wait("sync", "dv")
        c.wait("sync", "pe")
        c.dma("sync", "lds", sl(cons, 0, NT),
              io["cosbq" if wname == "wq_st" else "cosb"])
        c.dma("sync", "lds", sl(cons, NT, 2 * NT),
              io["sinbq" if wname == "wq_st" else "sinb"])
        for ce in range(16):
            for half, oc in ((0, ce), (1, ce + 16)):
                sslab = sl(slab, 4096 * half, 4096 * (half + 1))
                c.wait("gpsimd", "pe")
                c.dma("gpsimd", "ldg",
                      sslab.rearrange("p (a b) -> p a b", a=32),
                      io[wname][oc])
                c.wait("tensor", "ldg")
                c.wait("tensor", "dv")
                ps = qe_ps if half == 0 else qo_ps
                for dc in range(NC):
                    for p in range(2):
                        c.op("tensor", "pe", lambda e,
                             o=ps[:, 512 * p:512 * (p + 1)],
                             l=sslab[:, 128 * dc:128 * (dc + 1)],
                             r=bigA[:, NT * dc + 512 * p:NT * dc + 512 * (p + 1)],
                             s=(dc == 0), t=(dc == NC - 1):
                             e.matmul(o, l, r, start=s, stop=t))
            t1 = sl(scr, 0, NT)
            t2 = sl(scr, NT, 2 * NT)
            oe = sl(scrb, 0, NT)
            oo = sl(scrb, NT, 2 * NT)
            cosv = sl(cons, 0, NT)
            sinv = sl(cons, NT, 2 * NT)
            c.wait("vector", "lds")
            c.wait("vector", "pe")
            c.wait("vector", "sts")
            c.op("vector", "dv", lambda e: e.tensor_mul(t1, qe_ps[:], cosv))
            c.op("vector", "dv", lambda e: e.tensor_mul(t2, qo_ps[:], sinv))
            c.wait("vector", "dv")
            c.op("vector", "dv", lambda e: e.tensor_sub(oe, t1, t2))
            c.op("vector", "dv", lambda e: e.tensor_mul(t1, qe_ps[:], sinv))
            c.op("vector", "dv", lambda e: e.tensor_mul(t2, qo_ps[:], cosv))
            c.wait("vector", "dv")
            c.op("vector", "dv", lambda e: e.tensor_add(oo, t1, t2))
            c.wait("sync", "dv")
            c.dma("sync", "sts", dap(dst, 2 * 128 * ce * NT,
                                     [[2 * NT, 128], [1, NT]]), oe)
            c.dma("sync", "sts", dap(dst, (2 * 128 * ce + 1) * NT,
                                     [[2 * NT, 128], [1, NT]]), oo)
    v_ps = B["pC"][:, 0:256]
    for ob in range(16):
        vslab = sl(slab, 0, 8192)
        c.wait("gpsimd", "pe")
        c.dma("gpsimd", "ldg",
              vslab.rearrange("p (a b) -> p a b", a=32),
              io["wv_mv"][ob].transpose([1, 0, 2]))
        c.wait("tensor", "ldg")
        for tt in range(8):
            c.wait("tensor", "dv")
            for dc in range(NC):
                c.op("tensor", "pe", lambda e, o=v_ps,
                     l=bigA[:, NT * dc + 128 * tt:NT * dc + 128 * (tt + 1)],
                     r=vslab[:, 256 * dc:256 * (dc + 1)],
                     s=(dc == 0), t=(dc == NC - 1):
                     e.matmul(o, l, r, start=s, stop=t))
            vst = sl(scrb, 0, 256)
            c.wait("vector", "pe")
            c.wait("vector", "sts")
            c.op("vector", "dv", lambda e, o=vst: e.tensor_copy(o, v_ps))
            c.wait("sync", "dv")
            c.dma("sync", "sts",
                  dap(io["kv_in"], 4096 * NT + 128 * tt * D + 256 * ob,
                      [[D, 128], [1, 256]]), vst)


def ph_proj_acc(c, B, w_st, mov, name, extra_mms=None, evac=None):
    slab, scr = B["slab"], B["scr"]
    acc = c.io["acc"]
    pe_after = [c.tot("pe")] * 2
    dv_after = [c.tot("dv")] * 2
    acg_after = [c.tot("acg")] * 2
    ld_q = "gpsimd"
    c.wait("gpsimd", "acg")
    for oc in range(32):
        sn = oc % 2
        sslab = sl(slab, 4096 * sn, 4096 * (sn + 1))
        ps = [B["pA"], B["pB"]][sn]
        sem = "sl" + str(sn)
        c.wait(ld_q, "pe", pe_after[sn])
        ldt = c.dma(ld_q, sem, sslab.rearrange("p (a b) -> p a b", a=32),
                    w_st[oc])
        c.wait("tensor", sem, ldt)
        c.wait("tensor", "dv", dv_after[sn])
        for dc in range(NC):
            for p in range(2):
                c.op("tensor", "pe", lambda e, o=ps[:, 512 * p:512 * (p + 1)],
                     l=sslab[:, 128 * dc:128 * (dc + 1)],
                     r=mov[:, NT * dc + 512 * p:NT * dc + 512 * (p + 1)],
                     s=(dc == 0), t=(dc == NC - 1 and extra_mms is None):
                     e.matmul(o, l, r, start=s, stop=t))
        if extra_mms is not None:
            extra_mms(oc, ps)
        pe_after[sn] = c.tot("pe")
        out_f = sl(scr, NT * sn, NT * (sn + 1))
        c.wait("vector", "pe")
        c.wait("vector", "acg", acg_after[sn])
        if evac is None:
            c.op("vector", "dv", lambda e, o=out_f, p_=ps: e.tensor_copy(o, p_[:]))
        else:
            evac(oc, ps, out_f)
        dv_after[sn] = c.tot("dv")
        c.wait("gpsimd", "dv")
        acg_after[sn] = c.dma("gpsimd", "acg", acc[oc], out_f, accum_op=ALU.add)


def ph_route_top2(c, B, gates_sb, w_sb, eqc=None, ab_sb=None, dbg=None):
    sml = B["sml"]
    m1 = sml[:, 128:129]
    m2 = sml[:, 129:130]
    dd = sml[:, 130:132]
    gg = sml[:, 132:134]
    eq1 = sml[:, 136:144]
    eq2 = sml[:, 144:152]
    lm = sml[:, 152:160]
    t1 = sml[:, 160:168]
    t2 = sml[:, 168:176]
    c.wait("vector", "dv")
    c.wait("vector", "sts")
    c.wait("vector", "pe")
    c.op("vector", "dv", lambda e: e.reduce_max(m1, gates_sb, axis=AX.X))
    c.wait("vector", "dv")
    c.op("vector", "dv", lambda e: e.tensor_scalar(
        out=eq1, in0=gates_sb, scalar1=m1, scalar2=None, op0=ALU.is_equal))
    c.wait("vector", "dv")
    c.op("vector", "dv", lambda e: e.scalar_tensor_tensor(
        out=lm, in0=eq1, scalar=NEG, in1=gates_sb, op0=ALU.mult, op1=ALU.add))
    c.wait("vector", "dv")
    c.op("vector", "dv", lambda e: e.reduce_max(m2, lm, axis=AX.X))
    c.wait("vector", "dv")
    c.op("vector", "dv", lambda e: e.tensor_scalar(
        out=eq2, in0=lm, scalar1=m2, scalar2=None, op0=ALU.is_equal))
    c.op("vector", "dv", lambda e: e.tensor_sub(dd[:, 0:1], m1, m2))
    c.wait("vector", "dv")
    c.op("vector", "dv", lambda e: e.tensor_sub(dd[:, 1:2], m2, m1))
    c.wait("scalar", "dv")
    c.op("scalar", "sc", lambda e: e.activation(gg, dd, AF.Sigmoid))
    c.wait("vector", "sc")
    c.op("vector", "dv", lambda e: e.tensor_scalar(
        out=t1, in0=eq1, scalar1=gg[:, 0:1], scalar2=None, op0=ALU.mult))
    c.op("vector", "dv", lambda e: e.tensor_scalar(
        out=t2, in0=eq2, scalar1=gg[:, 1:2], scalar2=None, op0=ALU.mult))
    c.wait("vector", "dv")
    if ab_sb is not None:
        c.op("vector", "dv", lambda e: e.tensor_mul(t1, t1, ab_sb[:, 0:8]))
        c.op("vector", "dv", lambda e: e.tensor_mul(t2, t2, ab_sb[:, 8:16]))
        c.wait("vector", "dv")
    c.op("vector", "dv", lambda e: e.tensor_add(w_sb, t1, t2))
    if eqc is not None:
        c.op("vector", "dv", lambda e: e.tensor_copy(eqc[0], eq1))
        c.op("vector", "dv", lambda e: e.tensor_copy(eqc[1], eq2))


def ph_ffn0(c, B):
    bigA, slab, scr, scrb = (sl(B["bigX"], 0, NC * NT), B["slab"], B["scr"],
                             B["scrb"])
    stat, sml, smlb, consf = B["stat"], B["sml"], B["smlb"], B["consf"]
    io = c.io
    acc = io["acc"]
    lg_ps = B["pA"][:, 0:8]
    wT_ps = B["pB"][0:8, 0:128]
    wexp_ps = B["pC"]
    hl_ps = B["pD"]
    wr_sb = c.sbuf("wr_sb", [128, NC * 8], BF16)
    c.dma("gpsimd", "ldg",
          wr_sb[:].rearrange("p (a b) -> p a b", a=32),
          io["wrT"].rearrange("(a p) b -> a p b", p=128).transpose([1, 0, 2]))
    e8_sb = c.sbuf("e8_sb", [8, 128], F32)
    c.dma("sync", "lds", e8_sb[:], io["e8"])
    gsb = sml[:, 0:8]
    w_sb = sml[:, 16:24]
    wf = sml[:, 32:40]
    for tt in range(8):
        c.wait("tensor", "ldg")
        c.wait("tensor", "dv")
        for dc in range(NC):
            c.op("tensor", "pe", lambda e, o=lg_ps,
                 l=bigA[:, NT * dc + 128 * tt:NT * dc + 128 * (tt + 1)],
                 r=wr_sb[:, 8 * dc:8 * (dc + 1)],
                 s=(dc == 0), t=(dc == NC - 1):
                 e.matmul(o, l, r, start=s, stop=t))
        c.wait("vector", "pe")
        c.op("vector", "dv", lambda e: e.tensor_copy(gsb, lg_ps))
        if "lg0" in c.taps:
            c.wait("sync", "dv")
            c.dma("sync", "sts", c.tap_out["lg0"][:, 8 * tt:8 * (tt + 1)], gsb)
        ph_route_top2(c, B, gsb, w_sb,
                      dbg=(c.tap_out["rt0"][:, 40 * tt:40 * (tt + 1)]
                           if "rt0" in c.taps else None))
        if "w0" in c.taps:
            c.wait("sync", "dv")
            c.dma("sync", "sts", c.tap_out["w0"][:, 8 * tt:8 * (tt + 1)], w_sb)
        c.wait("vector", "dv")
        c.wait("vector", "pe")
        c.op("vector", "dv", lambda e: e.tensor_copy(wf, w_sb))
        c.wait("tensor", "dv")
        c.wait("tensor", "pe")
        c.op("tensor", "pe", lambda e: e.transpose(wT_ps, wf, consf[:, 0:128]))
        c.wait("vector", "pe")
        c.op("vector", "dv", lambda e, o=sl(stat, 128 * tt, 128 * (tt + 1))[0:8, :]:
             e.tensor_copy(o, wT_ps))
    c.wait("tensor", "dv")
    c.wait("tensor", "lds")
    for p in range(2):
        c.op("tensor", "pe", lambda e, o=wexp_ps[:, 512 * p:512 * (p + 1)],
             r=sl(stat, 512 * p, 512 * (p + 1))[0:8, :]:
             e.matmul(o, e8_sb[:], r, start=True, stop=True))
    la_sb = sl(slab, 4096, 8192)
    c.dma("gpsimd", "ldg", la_sb.rearrange("p (a b) -> p a b", a=32),
          io["aflatT"].transpose([1, 0, 2]))
    c.wait("tensor", "ldg")
    for dc in range(NC):
        for p in range(2):
            c.op("tensor", "pe", lambda e, o=hl_ps[:, 512 * p:512 * (p + 1)],
                 l=la_sb[:, 128 * dc:128 * (dc + 1)],
                 r=bigA[:, NT * dc + 512 * p:NT * dc + 512 * (p + 1)],
                 s=(dc == 0), t=(dc == NC - 1):
                 e.matmul(o, l, r, start=s, stop=t))
    hl_sb = sl(scrb, 0, NT)
    hw_sb = sl(scrb, NT, 2 * NT)
    c.wait("vector", "pe")
    c.op("vector", "dv", lambda e: e.tensor_copy(hl_sb, hl_ps[:]))
    c.wait("vector", "dv")
    c.op("vector", "dv", lambda e: e.scalar_tensor_tensor(
        out=hw_sb, in0=hl_sb, scalar=2.0, in1=wexp_ps[:], op0=ALU.mult,
        op1=ALU.mult))
    lb_sb = sl(slab, 8192, 12288)
    c.dma("gpsimd", "ldg", lb_sb, io["bflat"])
    bb_sb = c.sbuf("bb0_sb", [128, 32], F32)
    c.dma("sync", "lds", bb_sb[:], io["bbase"])
    c.wait("tensor", "ldg")
    c.wait("tensor", "lds")

    def lora_mms(oc, ps):
        for p in range(2):
            c.op("tensor", "pe", lambda e, o=ps[:, 512 * p:512 * (p + 1)],
                 l=lb_sb[:, 128 * oc:128 * (oc + 1)],
                 r=sl(scrb, NT + 512 * p, NT + 512 * (p + 1)):
                 e.matmul(o, l, r, start=False, stop=True))

    def bias_evac(oc, ps, out_f):
        c.op("vector", "dv", lambda e, o=out_f, b=bb_sb[:, oc:oc + 1], p_=ps:
             e.tensor_scalar(out=o, in0=p_[:], scalar1=b, scalar2=None,
                             op0=ALU.add))

    ph_proj_acc(c, B, io["wb_st"], bigA, "ffn0b", extra_mms=lora_mms,
                evac=bias_evac)


def ph_dequant(c, B):
    scr, scrb, sczb = B["scr"], B["scrb"], B["sczb"]
    io = c.io
    c.wait("vector", "acg")
    c.wait("vector", "pe")
    nib = c.sbuf("dq_nib", [128, 512], I32)
    zfb0 = None
    qwt = c.sbuf("dq_qwt", [128, 512], I32)
    z4b = c.sbuf("dq_z4", [86, 512], I32)
    zfb = c.sbuf("dq_zf", [86, 1024], F32)
    c.op("gpsimd", "gp", lambda e: e.memset(zfb[:], 0.0))
    c.op("gpsimd", "gp", lambda e: e.memset(nib[:], 0))
    c.wait("vector", "gp")
    scz_d = io["scz_d"]
    for wname, qwk, qzk, sck, lnfold in (
            ("qkv", "qkvqw", "qkvqz", "qkvsc", "ln1c"),
            ("o", "oqw", "oqz", "osc", None),
            ("gu", "guqw", "guqz", "gusc", None),
            ("dn", "dnqw", "dnqz", "dnsc", None)):
        NGK = QW[wname]["NGK"]
        NS = QW[wname]["slots"] * 128
        PC = NS // 8
        nblocks = [(i, min(512, NS - i)) for i in range(0, NS, 512)]
        if lnfold:
            ln_sb = c.sbuf(f"dq_ln_{wname}", [128, 32], F32)
            c.dma("sync", "lds", ln_sb[:], io[lnfold])
        for nbi, (nb0, nbw) in enumerate(nblocks):
            pcw = nbw // 8
            c.wait("sync", "dv")
            c.wait("sync", "sts")
            c.dma("sync", "lds", z4b[0:NGK, 0:pcw],
                  io[qzk][:, nb0 // 8:nb0 // 8 + pcw])
            c.dma("sync", "lds", zfb[0:NGK, 0:nbw], io[sck][:, nb0:nb0 + nbw])
            c.wait("vector", "lds")
            pranges = [(0, min(NGK, 64))]
            if NGK > 64:
                pranges.append((64, NGK))
            for (p0, p1) in pranges:
                for j in range(8):
                    c.op("vector", "dv", lambda e, j=j, pcw=pcw, p0=p0, p1=p1:
                         e.tensor_scalar(
                             out=nib[p0:p1, j:8 * pcw:8], in0=z4b[p0:p1, 0:pcw],
                             scalar1=ORDER[j], scalar2=15,
                             op0=ALU.logical_shift_right, op1=ALU.bitwise_and))
            c.wait("vector", "dv")
            for (p0, p1) in pranges:
                c.op("vector", "dv", lambda e, p0=p0, p1=p1, w=nbw:
                     e.tensor_copy(zfb[p0:p1, 512:512 + w], nib[p0:p1, 0:w]))
            c.wait("sync", "dv")
            c.dma("sync", "sts",
                  dap(scz_d, 1024 * nbi, [[6144, NGK], [1, 512]]),
                  zfb[0:NGK, 0:512])
            c.dma("sync", "sts",
                  dap(scz_d, 1024 * nbi + 512, [[6144, NGK], [1, 512]]),
                  zfb[0:NGK, 512:1024])
        c.wait("sync", "sts")
        w1_in = io["w1_in_" + W1BUF[wname]]
        base_el = W1OFF[wname]
        for g in range(NGK):
            c.wait("sync", "pe")
            c.wait("sync", "dv")
            c.dma("sync", "lds", qwt[:, 0:PC], io[qwk][g])
            for nbi, (nb0, nbw) in enumerate(nblocks):
                pcw = nbw // 8
                c.wait("sync", "dv")
                c.dma("sync", "lds", sczb[:],
                      dap(scz_d, g * 6144 + 1024 * nbi, [[0, 128], [1, 1024]]))
                if lnfold:
                    c.wait("vector", "lds")
                    c.op("vector", "dv", lambda e, g=g: e.tensor_scalar(
                        out=sczb[:, 0:512], in0=sczb[:, 0:512],
                        scalar1=ln_sb[:, g:g + 1], scalar2=None, op0=ALU.mult))
                c.wait("vector", "lds")
                c.wait("vector", "sts")
                for j in range(8):
                    c.op("vector", "dv", lambda e, j=j, w=pcw, b=nb0 // 8:
                         e.tensor_scalar(
                             out=nib[:, j:8 * w:8], in0=qwt[:, b:b + w],
                             scalar1=ORDER[j], scalar2=15,
                             op0=ALU.logical_shift_right, op1=ALU.bitwise_and))
                c.wait("vector", "dv")
                c.op("vector", "dv", lambda e, w=nbw: e.tensor_copy(
                    sl(scr, 0, w), nib[:, 0:w]))
                c.wait("vector", "dv")
                c.op("vector", "dv", lambda e, w=nbw: e.tensor_sub(
                    sl(scr, 0, w), sl(scr, 0, w), sczb[:, 512:512 + w]))
                c.wait("vector", "dv")
                c.op("vector", "dv", lambda e, w=nbw: e.tensor_mul(
                    sl(scrb, 0, w), sl(scr, 0, w), sczb[:, 0:w]))
                c.wait("sync", "dv")
                s0 = nb0 // 128
                ns = nbw // 128
                c.dma("sync", "sts",
                      dap(w1_in, base_el + s0 * (128 * NGK * 128) + g * 128,
                          [[NGK * 128, 128], [128 * NGK * 128, ns], [1, 128]]),
                      sl(scrb, 0, nbw).rearrange("p (s n) -> p s n", n=128))


def ph_attn(c, B):
    bigA, slab, scr = sl(B["bigX"], 0, NC * NT), B["slab"], B["scr"]
    maskf, probT, rowf = B["maskf"], B["probT"], B["rowf"]
    onesb, onesf = B["onesb"], B["onesf"]
    io = c.io
    kv_all = io["kv_all"]
    c.wait("sync", "cc", 1)
    sc_ps, o_ps, rl_ps = B["pA"], B["pB"], B["pD"]
    l_ps = B["pC"][0:1, :]
    pe_after = [c.tot("pe")] * 2
    for h in range(NH):
        sn = h % 2
        kbase = 9216 * sn
        q_sb = sl(slab, kbase + 8192, kbase + 9216)
        sem = "sl" + str(sn)
        c.wait("sync", "pe", pe_after[sn])
        for s2 in range(4):
            c.dma("sync", sem, sl(slab, kbase + 1024 * s2, kbase + 1024 * (s2 + 1)),
                  kv_all[8192 * s2 + 128 * h:8192 * s2 + 128 * (h + 1), :])
        for s2 in range(4):
            c.dma("sync", sem,
                  sl(slab, kbase + 4096 + 1024 * s2,
                     kbase + 4096 + 1024 * (s2 + 1)).rearrange(
                      "p (b i) -> p b i", b=8),
                  dap(kv_all, (8192 * s2 + 4096) * NT + 128 * h,
                      [[D, 128], [128 * D, 8], [1, 128]]))
        ldt = c.dma("sync", sem, q_sb, io["q_d"][128 * h:128 * (h + 1), :])
        c.wait("tensor", sem, ldt)
        c.wait("tensor", "dv")
        for kb in range(8):
            if kb < 4:
                pieces = [(128 * kb, 512 - 128 * kb, 0), (512, 512, 1)]
            else:
                pieces = [(128 * kb, 1024 - 128 * kb, 1)]
            for s2 in range(4):
                first = (kb == 0 and s2 == 0)
                c.wait("tensor", "sc")
                for (q0, qw, bank) in pieces:
                    c.op("tensor", "pe", lambda e, o=sc_ps[:, q0:q0 + qw],
                         l=sl(slab, kbase + 1024 * s2 + 128 * kb,
                              kbase + 1024 * s2 + 128 * (kb + 1)),
                         r=q_sb[:, q0:q0 + qw]:
                         e.matmul(o, l, r, start=True, stop=True))
                c.wait("vector", "pe")
                c.wait("vector", "sc")
                c.op("vector", "dv", lambda e, q0=128 * kb, s2=s2:
                     e.tensor_add(sc_ps[:, q0:q0 + 128], sc_ps[:, q0:q0 + 128],
                                  sl(maskf, 128 * s2, 128 * (s2 + 1))))
                c.wait("scalar", "dv")
                c.wait("scalar", "pe")
                c.wait("scalar", "sc")
                q0 = 128 * kb
                c.op("scalar", "sc", lambda e, q0=q0:
                     e.activation(probT[:, q0:1024], sc_ps[:, q0:1024],
                                  AF.Exp))
                c.wait("tensor", "sc")
                for (q0p, qwp, bank) in pieces:
                    stop_l = ((bank == 0 and kb == 3 and s2 == 3) or
                              (bank == 1 and kb == 7 and s2 == 3))
                    c.op("tensor", "pe", lambda e, o=l_ps[0:1, q0p:q0p + qwp],
                         r=probT[:, q0p:q0p + qwp], s=first, t=stop_l:
                         e.matmul(o, onesb[:], r, start=s, stop=t))
                    c.op("tensor", "pe", lambda e, o=o_ps[:, q0p:q0p + qwp],
                         l=sl(slab, kbase + 4096 + 1024 * s2 + 128 * kb,
                              kbase + 4096 + 1024 * s2 + 128 * (kb + 1)),
                         r=probT[:, q0p:q0p + qwp], s=first, t=stop_l:
                         e.matmul(o, l, r, start=s, stop=t))
        lrow = rowf[0:1, 0:1024]
        rrow = rowf[0:1, 1024:2048]
        c.wait("vector", "pe")
        c.op("vector", "dv", lambda e: e.tensor_copy(lrow, l_ps))
        c.wait("vector", "dv")
        c.op("vector", "dv", lambda e: e.reciprocal(rrow, lrow))
        c.wait("tensor", "dv")
        for p in range(2):
            c.op("tensor", "pe", lambda e, o=rl_ps[:, 512 * p:512 * (p + 1)],
                 r=rowf[0:1, 1024 + 512 * p:1024 + 512 * (p + 1)]:
                 e.matmul(o, onesf[:], r, start=True, stop=True))
        rl_sb = sl(scr, 0, NT)
        c.wait("vector", "pe")
        c.op("vector", "dv", lambda e: e.tensor_copy(rl_sb, rl_ps[:]))
        c.wait("vector", "dv")
        c.op("vector", "dv", lambda e, o=bigA[:, NT * h:NT * (h + 1)]:
             e.tensor_mul(o, o_ps[:], rl_sb))
        pe_after[sn] = c.tot("pe")


def ph_ffn1(c, B):
    bigX, slab, scr, scrb = B["bigX"], B["slab"], B["scr"], B["scrb"]
    stat, sml, smlb, consf = B["stat"], B["sml"], B["smlb"], B["consf"]
    rowf, onesb, onesf = B["rowf"], B["onesb"], B["onesf"]
    io = c.io
    acc = io["acc"]
    xn1 = sl(bigX, 0, NC * NT)
    gw1_sb = c.sbuf("gw1_sb", [128, NC * 32], BF16)
    c.dma("gpsimd", "ldg",
          gw1_sb[:].rearrange("p (a b) -> p a b", a=32),
          io["gw1T"].rearrange("(a p) b -> a p b", p=128).transpose([1, 0, 2]))
    gw2_sb = c.sbuf("gw2_sb", [32, 8], BF16)
    c.dma("gpsimd", "ldg", gw2_sb[:], io["gw2T"])
    e8_sb2 = c.sbuf("e8_sb2", [8, 128], F32)
    c.dma("sync", "lds", e8_sb2[:], io["e8"])
    brows = c.sbuf("brows", [128, 112], F32)
    c.dma("sync", "lds", brows[:, 0:32], dap(io["gb1r"], 0, [[0, 128], [1, 32]]))
    c.dma("sync", "lds", brows[:, 32:64], dap(io["lngr"], 0, [[0, 128], [1, 32]]))
    c.dma("sync", "lds", brows[:, 64:96], dap(io["lnbr"], 0, [[0, 128], [1, 32]]))
    c.dma("sync", "lds", brows[:, 96:104], dap(io["gb2r"], 0, [[0, 128], [1, 8]]))
    wbuf = c.sbuf("wbuf", [128, 256], F32)
    hnT_sb = c.sbuf("hnT_sb", [32, 1024], BF16)
    lg1 = B["pA"][:, 0:32]
    gate_ps = B["pC"][:, 0:8]
    hnT_ps = B["pB"][0:32, 0:128]
    c1_ps = B["pD"][0:8, 0:1]
    c2_ps = B["pD"][0:8, 512:513]
    h1g = sml[:, 0:32]
    mu = sml[:, 32:33]
    var = sml[:, 33:34]
    rstd = sml[:, 34:35]
    xc = sml[:, 40:72]
    sq = sml[:, 72:104]
    hn = sml[:, 104:136]
    gsb = sml[:, 8:16]
    hnb = smlb[:, 0:32]
    for tt in range(8):
        c.wait("tensor", "ldg")
        c.wait("tensor", "dv")
        for dc in range(NC):
            c.op("tensor", "pe", lambda e, o=lg1,
                 l=xn1[:, NT * dc + 128 * tt:NT * dc + 128 * (tt + 1)],
                 r=gw1_sb[:, 32 * dc:32 * (dc + 1)],
                 s=(dc == 0), t=(dc == NC - 1):
                 e.matmul(o, l, r, start=s, stop=t))
        c.wait("vector", "pe")
        c.wait("vector", "lds")
        c.wait("vector", "sc")
        c.op("vector", "dv", lambda e: e.tensor_add(h1g, lg1, brows[:, 0:32]))
        c.wait("vector", "dv")
        c.op("vector", "dv", lambda e: e.reduce_sum(mu, h1g, axis=AX.X))
        c.wait("vector", "dv")
        c.op("vector", "dv", lambda e: e.tensor_scalar(
            out=mu, in0=mu, scalar1=1.0 / 32, scalar2=None, op0=ALU.mult))
        c.wait("vector", "dv")
        c.op("vector", "dv", lambda e: e.tensor_scalar(
            out=xc, in0=h1g, scalar1=mu, scalar2=None, op0=ALU.subtract))
        c.wait("vector", "dv")
        c.op("vector", "dv", lambda e: e.tensor_mul(sq, xc, xc))
        c.wait("vector", "dv")
        c.op("vector", "dv", lambda e: e.reduce_sum(var, sq, axis=AX.X))
        c.wait("vector", "dv")
        c.op("vector", "dv", lambda e: e.tensor_scalar(
            out=rstd, in0=var, scalar1=1.0 / 32, scalar2=1e-5,
            op0=ALU.mult, op1=ALU.add))
        c.wait("scalar", "dv")
        c.op("scalar", "sc", lambda e: e.activation(rstd, rstd, AF.Sqrt))
        c.wait("vector", "sc")
        c.op("vector", "dv", lambda e: e.reciprocal(rstd, rstd))
        c.wait("vector", "dv")
        c.op("vector", "dv", lambda e: e.tensor_scalar(
            out=hn, in0=xc, scalar1=rstd, scalar2=None, op0=ALU.mult))
        c.wait("vector", "dv")
        c.op("vector", "dv", lambda e: e.tensor_mul(hn, hn, brows[:, 32:64]))
        c.wait("vector", "dv")
        c.op("vector", "dv", lambda e: e.tensor_add(hn, hn, brows[:, 64:96]))
        c.wait("vector", "dv")
        c.op("vector", "dv", lambda e: e.tensor_relu(hn, hn))
        c.wait("vector", "dv")
        c.op("vector", "dv", lambda e: e.tensor_copy(hnb, hn))
        c.wait("tensor", "dv")
        c.wait("tensor", "sc")
        c.op("tensor", "pe", lambda e: e.transpose(hnT_ps, hn, consf[:, 0:128]))
        c.wait("vector", "pe")
        c.op("vector", "dv", lambda e, o=hnT_sb[:, 128 * tt:128 * (tt + 1)]:
             e.tensor_copy(o, hnT_ps))
        c.wait("tensor", "dv")
        c.op("tensor", "pe", lambda e, l=hnT_sb[:, 128 * tt:128 * (tt + 1)]:
             e.matmul(gate_ps, l, gw2_sb[:], start=True, stop=True))
        c.wait("vector", "pe")
        c.op("vector", "dv", lambda e: e.tensor_add(gsb, gate_ps, brows[:, 96:104]))
        eqc = (smlb[:, 32:40], smlb[:, 40:48])
        ph_route_top2(c, B, gsb, wbuf[:, 32 * tt:32 * tt + 8], eqc=eqc)
        c.wait("vector", "dv")
        c.op("vector", "dv", lambda e, o=wbuf[:, 32 * tt + 16:32 * tt + 18]:
             e.tensor_copy(o, sml[:, 132:134]))
        c.op("vector", "dv", lambda e, o=wbuf[:, 32 * tt + 8:32 * tt + 16]:
             e.tensor_copy(o, smlb[:, 40:48]))
        c.op("vector", "dv", lambda e, o=wbuf[:, 32 * tt + 24:32 * tt + 32]:
             e.tensor_copy(o, smlb[:, 32:40]))
        c.wait("tensor", "dv")
        c.op("tensor", "pe", lambda e, s=(tt == 0), t=(tt == 7):
             e.matmul(c1_ps, smlb[:, 32:40], onesb[:], start=s, stop=t))
        c.op("tensor", "pe", lambda e, s=(tt == 0), t=(tt == 7):
             e.matmul(c2_ps, smlb[:, 40:48], onesb[:], start=s, stop=t))
    cnt_sb = sml[:, 144:146]
    c.wait("vector", "pe")
    c.op("vector", "dv", lambda e: e.tensor_copy(cnt_sb[0:8, 0:1], c1_ps))
    c.op("vector", "dv", lambda e: e.tensor_copy(cnt_sb[0:8, 1:2], c2_ps))
    c.wait("sync", "dv")
    c.dma("sync", "sts", io["cnt_in"], cnt_sb[0:8, :])
    c.wait("gpsimd", "sts")
    ccs = c.sem("cc")
    cc_t = c.inc("cc")
    c.em("gpsimd", lambda e, s=ccs: e.collective_compute(
        "AllReduce", ALU.add, replica_groups=[[0, 1, 2, 3, 4, 5, 6, 7]],
        ins=[io["cnt_in"]], outs=[io["cnt_out"]]).then_inc(s, 1))
    c.wait("sync", "cc", cc_t)
    c.dma("sync", "lds", cnt_sb[0:8, :], io["cnt_out"])
    if "counts" in c.taps:
        c.wait("sync", "lds")
        c.dma("sync", "sts", c.tap_out["counts"], cnt_sb[0:8, :])
    al_sb = sml[:, 146:148]
    c.wait("vector", "lds")
    c.op("vector", "dv", lambda e: e.tensor_scalar(
        out=al_sb[0:8, :], in0=cnt_sb[0:8, :], scalar1=3072.0, scalar2=None,
        op0=ALU.is_le))
    c.wait("sync", "dv")
    c.dma("sync", "sts", dap(io["cnt_in"], 0, [[1, 8]]), al_sb[0:8, 0:1])
    c.dma("sync", "sts", dap(io["cnt_in"], 8, [[1, 8]]), al_sb[0:8, 1:2])
    c.wait("sync", "sts")
    ab_sb = sml[:, 160:176]
    c.dma("sync", "lds", ab_sb[:, 0:8], dap(io["cnt_in"], 0, [[0, 128], [1, 8]]))
    c.dma("sync", "lds", ab_sb[:, 8:16], dap(io["cnt_in"], 8, [[0, 128], [1, 8]]))
    wT_ps = B["pB"][0:8, 0:128]
    wf = sml[:, 176:184]
    w2 = sml[:, 184:192]
    t1 = sml[:, 192:200]
    t2 = sml[:, 200:208]
    for tt in range(8):
        c.wait("vector", "lds")
        c.wait("vector", "dv")
        c.op("vector", "dv", lambda e, i=wbuf[:, 32 * tt + 24:32 * tt + 32],
             s1=wbuf[:, 32 * tt + 16:32 * tt + 17]:
             e.tensor_scalar(out=t1, in0=i, scalar1=s1, scalar2=None,
                             op0=ALU.mult))
        c.op("vector", "dv", lambda e, i=wbuf[:, 32 * tt + 8:32 * tt + 16],
             s1=wbuf[:, 32 * tt + 17:32 * tt + 18]:
             e.tensor_scalar(out=t2, in0=i, scalar1=s1, scalar2=None,
                             op0=ALU.mult))
        c.wait("vector", "dv")
        c.op("vector", "dv", lambda e: e.tensor_mul(t1, t1, ab_sb[:, 0:8]))
        c.op("vector", "dv", lambda e: e.tensor_mul(t2, t2, ab_sb[:, 8:16]))
        c.wait("vector", "dv")
        c.op("vector", "dv", lambda e: e.tensor_add(w2, t1, t2))
        if "w1route" in c.taps:
            c.wait("sync", "dv")
            c.dma("sync", "sts", c.tap_out["w1route"][:, 8 * tt:8 * (tt + 1)], w2)
        c.wait("vector", "dv")
        c.wait("vector", "pe")
        c.wait("vector", "sts")
        c.op("vector", "dv", lambda e: e.tensor_copy(wf, w2))
        c.wait("tensor", "dv")
        c.wait("tensor", "pe")
        c.op("tensor", "pe", lambda e: e.transpose(wT_ps, wf, consf[:, 0:128]))
        c.wait("vector", "pe")
        c.op("vector", "dv", lambda e, o=sl(stat, 128 * tt, 128 * (tt + 1))[0:8, :]:
             e.tensor_copy(o, wT_ps))
    wexp_ps = B["pC"]
    hl_ps = B["pD"]
    c.wait("tensor", "dv")
    c.wait("tensor", "lds")
    for p in range(2):
        c.op("tensor", "pe", lambda e, o=wexp_ps[:, 512 * p:512 * (p + 1)],
             r=sl(stat, 512 * p, 512 * (p + 1))[0:8, :]:
             e.matmul(o, e8_sb2[:], r, start=True, stop=True))
    la_sb = sl(slab, 4096, 8192)
    c.dma("gpsimd", "ldg", la_sb.rearrange("p (a b) -> p a b", a=32),
          io["laT"].transpose([1, 0, 2]))
    c.wait("tensor", "ldg")
    for dc in range(NC):
        for p in range(2):
            c.op("tensor", "pe", lambda e, o=hl_ps[:, 512 * p:512 * (p + 1)],
                 l=la_sb[:, 128 * dc:128 * (dc + 1)],
                 r=xn1[:, NT * dc + 512 * p:NT * dc + 512 * (p + 1)],
                 s=(dc == 0), t=(dc == NC - 1):
                 e.matmul(o, l, r, start=s, stop=t))
    hl_sb = sl(scrb, 0, NT)
    hw_sb = sl(scrb, NT, 2 * NT)
    c.wait("vector", "pe")
    c.op("vector", "dv", lambda e: e.tensor_copy(hl_sb, hl_ps[:]))
    c.wait("vector", "dv")
    c.op("vector", "dv", lambda e: e.scalar_tensor_tensor(
        out=hw_sb, in0=hl_sb, scalar=1.0 / 16, in1=wexp_ps[:], op0=ALU.mult,
        op1=ALU.mult))
    lb_sb = sl(slab, 8192, 12288)
    c.dma("gpsimd", "ldg", lb_sb, io["lbflat"])
    c.wait("tensor", "ldg")

    def lora_mms1(oc, ps):
        for p in range(2):
            c.op("tensor", "pe", lambda e, o=ps[:, 512 * p:512 * (p + 1)],
                 l=lb_sb[:, 128 * oc:128 * (oc + 1)],
                 r=sl(scrb, NT + 512 * p, NT + 512 * (p + 1)):
                 e.matmul(o, l, r, start=False, stop=True))

    ph_proj_acc(c, B, io["w1f_st"], xn1, "ffn1m", extra_mms=lora_mms1)


def w1slab(c, io, wname, oc):
    wkey = "gu" if wname == "gu_up" else wname
    NGK = QW[wkey]["NGK"]
    nm = W1BUF[wkey]
    rank, slot = oc % 8, oc // 8
    if wname == "gu_up":
        slot = 11 + oc // 8
    base = W1OFF[wkey]
    per = W1A_PER_RANK if nm == "a" else W1G_PER_RANK
    t = io["w1_all_" + nm]
    off = rank * per + base + slot * (128 * NGK * 128)
    return dap(t, off, [[NGK * 128, 128], [1, NGK * 128]])


def ph_l1qkvo(c, B):
    bigX, slab, scr, scrb = B["bigX"], B["slab"], B["scr"], B["scrb"]
    stat, rowf, onesb, onesf = B["stat"], B["rowf"], B["onesb"], B["onesf"]
    io = c.io
    acc = io["acc"]
    xn1 = sl(bigX, 0, NC * NT)
    c.wait("sync", "cc", 3)
    ss_ps = B["pA"][0:1, :]
    for dc in range(NC):
        sqb = sl(scrb, 0, NT)
        c.wait("scalar", "pe")
        c.wait("scalar", "dv")
        c.op("scalar", "sc", lambda e, o=sqb, i=xn1[:, NT * dc:NT * (dc + 1)]:
             e.activation(o, i, AF.Square))
        c.wait("tensor", "sc")
        for p in range(2):
            c.op("tensor", "pe", lambda e, o=ss_ps[0:1, 512 * p:512 * (p + 1)],
                 r=sl(scrb, 512 * p, 512 * (p + 1)), s=(dc == 0),
                 t=(dc == NC - 1): e.matmul(o, onesb[:], r, start=s, stop=t))
    rr = rowf[0:1, 0:NT]
    c.wait("vector", "pe")
    c.op("vector", "dv", lambda e: e.tensor_scalar(
        out=rr, in0=ss_ps, scalar1=1.0 / D, scalar2=1e-6,
        op0=ALU.mult, op1=ALU.add))
    c.wait("scalar", "dv")
    c.op("scalar", "sc", lambda e: e.activation(rr, rr, AF.Sqrt))
    c.wait("vector", "sc")
    c.op("vector", "dv", lambda e: e.reciprocal(rr, rr))
    c.wait("sync", "dv")
    c.dma("sync", "sts", dap(io["stat_d"], 0, [[1, NT]]), rr)
    c.wait("sync", "sts")
    rstd1b = sl(stat, 0, NT)
    c.dma("sync", "lds", rstd1b, dap(io["stat_d"], 0, [[0, 128], [1, NT]]))
    pe_after = [c.tot("pe")] * 2
    dv_after = [c.tot("dv")] * 2
    sts_after = [c.tot("sts")] * 2
    c.wait("vector", "lds")
    for oc in range(32):
        sn = oc % 2
        sslab = sl(slab, 4096 * sn, 4096 * (sn + 1))
        ps = [B["pA"], B["pB"]][sn]
        sem = "sl" + str(sn)
        c.wait("sync", "pe", pe_after[sn])
        ldt = c.dma("sync", sem, sslab, w1slab(c, io, "qkv", oc))
        c.wait("tensor", sem, ldt)
        c.wait("tensor", "dv", dv_after[sn])
        for dc in range(NC):
            for p in range(2):
                c.op("tensor", "pe", lambda e, o=ps[:, 512 * p:512 * (p + 1)],
                     l=sslab[:, 128 * dc:128 * (dc + 1)],
                     r=xn1[:, NT * dc + 512 * p:NT * dc + 512 * (p + 1)],
                     s=(dc == 0), t=(dc == NC - 1):
                     e.matmul(o, l, r, start=s, stop=t))
        pe_after[sn] = c.tot("pe")
        hqb = sl(scrb, NT * sn, NT * (sn + 1))
        c.wait("vector", "pe")
        c.wait("vector", "sts", sts_after[sn])
        c.op("vector", "dv", lambda e, o=hqb, p_=ps: e.tensor_mul(o, p_[:], rstd1b))
        dv_after[sn] = c.tot("dv")
        c.wait("sync", "dv")
        sts_after[sn] = c.dma("sync", "sts", io["hq_d"][oc], hqb)
    c.wait("sync", "sts")
    c.wait("sync", "pe")
    c.wait("sync", "dv")
    c.wait("sync", "sc")
    for i in range(8):
        c.dma("sync", "lds",
              sl(bigX, 4 * NT * i, 4 * NT * (i + 1)).rearrange(
                  "p (a b) -> p a b", a=4),
              io["hq_d"][4 * i:4 * (i + 1)].transpose([1, 0, 2]))
    hq = sl(bigX, 0, NC * NT)
    ln2_sb = c.sbuf("ln2_sb", [128, 32], F32)
    c.dma("sync", "lds", ln2_sb[:], io["ln2c"])
    ss2_ps = B["pC"][0:1, :]
    c.wait("gpsimd", "acg")
    pe_after = [c.tot("pe")] * 2
    dv_after = [c.tot("dv")] * 2
    for oc in range(32):
        sn = oc % 2
        sslab = sl(slab, 4096 * sn, 4096 * (sn + 1))
        ps = [B["pA"], B["pB"]][sn]
        sem = "sl" + str(sn)
        xn1c = sl(scrb, NT, 2 * NT)
        c.wait("sync", "pe", pe_after[sn])
        c.wait("sync", "dv")
        ldt = c.dma("sync", sem, sslab, w1slab(c, io, "o", oc))
        c.dma("sync", "lds", xn1c, io["xn1_d"][oc])
        c.wait("tensor", sem, ldt)
        c.wait("tensor", "lds")
        c.wait("tensor", "dv", dv_after[sn])
        c.wait("tensor", "sc")
        for dc in range(NC):
            for p in range(2):
                c.op("tensor", "pe", lambda e, o=ps[:, 512 * p:512 * (p + 1)],
                     l=sslab[:, 128 * dc:128 * (dc + 1)],
                     r=hq[:, NT * dc + 512 * p:NT * dc + 512 * (p + 1)],
                     s=(dc == 0), t=(dc == NC - 1):
                     e.matmul(o, l, r, start=s, stop=t))
        pe_after[sn] = c.tot("pe")
        of = sl(scr, 0, NT)
        h1f = sl(scr, NT, 2 * NT)
        c.wait("vector", "pe")
        c.wait("vector", "acg")
        c.wait("vector", "lds")
        c.wait("vector", "sc")
        c.op("vector", "dv", lambda e, o=of, p_=ps: e.tensor_copy(o, p_[:]))
        c.wait("gpsimd", "dv")
        c.dma("gpsimd", "acg", acc[oc], of, accum_op=ALU.add)
        c.op("vector", "dv", lambda e, o=h1f, p_=ps, x_=xn1c: e.tensor_add(o, p_[:], x_))
        c.wait("scalar", "dv")
        sqb = sl(scrb, 0, NT)
        c.op("scalar", "sc", lambda e, o=sqb, i=h1f: e.activation(o, i, AF.Square))
        c.wait("tensor", "sc")
        for p in range(2):
            c.op("tensor", "pe", lambda e, o=ss2_ps[0:1, 512 * p:512 * (p + 1)],
                 r=sl(scrb, 512 * p, 512 * (p + 1)), s=(oc == 0),
                 t=(oc == NC - 1): e.matmul(o, onesb[:], r, start=s, stop=t))
        c.wait("vector", "pe")
        c.wait("vector", "dv")
        c.wait("vector", "sc")
        c.op("vector", "dv", lambda e, o=h1f, b=ln2_sb[:, oc:oc + 1]:
             e.tensor_scalar(out=o, in0=h1f, scalar1=b, scalar2=None,
                             op0=ALU.mult))
        h1sbf = sl(scrb, 0, NT)
        c.wait("vector", "dv")
        c.op("vector", "dv", lambda e, o=h1sbf, i=h1f: e.tensor_copy(o, i))
        dv_after[sn] = c.tot("dv")
        c.wait("sync", "dv")
        c.dma("sync", "sts", io["h1s_d"][oc], h1sbf)
    rr2 = rowf[0:1, NT:2 * NT]
    c.wait("vector", "pe")
    c.op("vector", "dv", lambda e: e.tensor_scalar(
        out=rr2, in0=ss2_ps, scalar1=1.0 / D, scalar2=1e-6,
        op0=ALU.mult, op1=ALU.add))
    c.wait("scalar", "dv")
    c.op("scalar", "sc", lambda e: e.activation(rr2, rr2, AF.Sqrt))
    c.wait("vector", "sc")
    c.op("vector", "dv", lambda e: e.reciprocal(rr2, rr2))
    c.wait("sync", "dv")
    c.dma("sync", "sts", dap(io["stat_d"], 2048, [[1, NT]]), rr2)
    c.wait("sync", "sts")
    c.dma("sync", "lds", B["rstd2b"][:],
          dap(io["stat_d"], 2048, [[0, 128], [1, NT]]))


def ph_l1gudn(c, B):
    bigX, slab, scr, scrb = B["bigX"], B["slab"], B["scr"], B["scrb"]
    rstd2b = B["rstd2b"]
    io = c.io
    acc = io["acc"]
    NGU = 86
    h1s = sl(slab, 8192, 8192 + 16384)
    gslab = sl(slab, 0, 4096)
    uslab = sl(slab, 4096, 8192)
    gps = B["pA"][:, 0:512]
    ups = B["pB"][:, 0:512]
    dps = B["pC"][:, 0:512]
    for half in range(2):
        rsl = rstd2b[:, 512 * half:512 * (half + 1)]
        c.wait("sync", "pe")
        c.wait("sync", "dv")
        c.wait("sync", "sts")
        for i in range(4):
            c.dma("sync", "lds",
                  sl(slab, 8192 + 4096 * i, 8192 + 4096 * (i + 1)).rearrange(
                      "p (a b) -> p a b", a=8),
                  dap(io["h1s_d"], (8 * i) * 128 * NT + 512 * half,
                      [[NT, 128], [128 * NT, 8], [1, 512]]))
        for gc in range(NGU):
            c.wait("sync", "pe")
            c.dma("sync", "lds", gslab, w1slab(c, io, "gu", gc))
            c.dma("sync", "lds", uslab, w1slab(c, io, "gu_up", gc))
            c.wait("tensor", "lds")
            c.wait("tensor", "sc")
            c.wait("tensor", "dv")
            for dc in range(NC):
                c.op("tensor", "pe", lambda e, o=gps,
                     l=gslab[:, 128 * dc:128 * (dc + 1)],
                     r=h1s[:, 512 * dc:512 * (dc + 1)],
                     s=(dc == 0), t=(dc == NC - 1):
                     e.matmul(o, l, r, start=s, stop=t))
            for dc in range(NC):
                c.op("tensor", "pe", lambda e, o=ups,
                     l=uslab[:, 128 * dc:128 * (dc + 1)],
                     r=h1s[:, 512 * dc:512 * (dc + 1)],
                     s=(dc == 0), t=(dc == NC - 1):
                     e.matmul(o, l, r, start=s, stop=t))
            t1 = sl(scr, 0, 512)
            sg = sl(scr, 512, 1024)
            t2 = sl(scr, 1024, 1536)
            c.wait("vector", "pe")
            c.wait("vector", "sc")
            c.wait("vector", "lds")
            c.op("vector", "dv", lambda e, r=rsl: e.tensor_mul(t1, gps, r))
            c.wait("scalar", "dv")
            c.op("scalar", "sc", lambda e: e.activation(sg, t1, AF.Silu))
            c.wait("vector", "sc")
            c.op("vector", "dv", lambda e, r=rsl: e.tensor_mul(t2, ups, r))
            c.wait("vector", "dv")
            c.wait("vector", "sts")
            c.op("vector", "dv", lambda e, o=sl(bigX, 512 * gc, 512 * (gc + 1)):
                 e.tensor_mul(o, sg, t2))
        if "hid1" in c.taps:
            for e_ in ("dv", "pe", "sc"):
                c.wait("sync", e_)
            for kc in range(NGU):
                c.dma("sync", "sts",
                      c.tap_out["hid1"][128 * kc:128 * (kc + 1),
                                        512 * half:512 * (half + 1)],
                      sl(bigX, 512 * kc, 512 * (kc + 1)))
        c.wait("gpsimd", "acg")
        pe_after = [c.tot("pe")] * 2
        dv_after = [c.tot("dv")] * 2
        for oc in range(32):
            sn = oc % 2
            dnslab = sl(slab, 11008 * sn, 11008 * (sn + 1))
            dps2 = [B["pC"][:, 0:512], B["pD"][:, 0:512]][sn]
            sem = "sl" + str(sn)
            c.wait("sync", "pe", pe_after[sn])
            ldt = c.dma("sync", sem, dnslab, w1slab(c, io, "dn", oc))
            c.wait("tensor", sem, ldt)
            c.wait("tensor", "dv", dv_after[sn])
            for kc in range(NGU):
                c.op("tensor", "pe", lambda e, o=dps2,
                     l=dnslab[:, 128 * kc:128 * (kc + 1)],
                     r=sl(bigX, 512 * kc, 512 * (kc + 1)),
                     s=(kc == 0), t=(kc == NGU - 1):
                     e.matmul(o, l, r, start=s, stop=t))
            pe_after[sn] = c.tot("pe")
            out_f = sl(scr, 512 * sn, 512 * (sn + 1))
            c.wait("vector", "pe")
            c.wait("vector", "acg")
            c.op("vector", "dv", lambda e, o=out_f, p_=dps2: e.tensor_copy(o, p_))
            dv_after[sn] = c.tot("dv")
            c.wait("gpsimd", "dv")
            c.dma("gpsimd", "acg", acc[oc][:, 512 * half:512 * (half + 1)],
                  out_f, accum_op=ALU.add)



def prep_st(W):
    return np.ascontiguousarray(
        W.reshape(32, 128, 32, 128).transpose(0, 3, 2, 1))


def evens_odds_perm():
    p = np.arange(D).reshape(-1, 2)
    return np.concatenate([p[:, 0], p[:, 1]])


def core_positions(j):
    m = np.arange(8)
    tiles = j + 4 * m
    return (tiles[:, None] * 128 + np.arange(128)[None, :]).reshape(-1)


def build_inputs(d):
    x = d["L0_attn_x"]
    perm = evens_odds_perm()
    shared = dict(
        wq_st=prep_st(np.ascontiguousarray(d["L0_attn_wq"][perm])),
        wk_st=prep_st(np.ascontiguousarray(d["L0_attn_wk"][perm])),
        wo_st=prep_st(d["L0_attn_wo"]),
        wb_st=prep_st(d["L0_ffn_W_base"]),
        w1f_st=prep_st(d["L1_ffn_weight"]),
        wv_mv=np.ascontiguousarray(
            d["L0_attn_wv"].reshape(16, 256, 32, 128).transpose(0, 2, 3, 1)),
        wrT=np.ascontiguousarray(d["L0_ffn_W_router"].T),
        aflatT=np.ascontiguousarray(
            d["L0_ffn_A"].transpose(2, 0, 1).reshape(NC, 128, 128)),
        bflat=np.ascontiguousarray(
            d["L0_ffn_B"].transpose(0, 2, 1).reshape(128, D)),
        bbase=np.ascontiguousarray(d["L0_ffn_b_base"].reshape(32, 128).T),
        laT=np.ascontiguousarray(
            d["L1_ffn_lora_A"].transpose(1, 0, 2).reshape(NC, 128, 128)),
        lbflat=np.ascontiguousarray(d["L1_ffn_lora_B"].reshape(128, D)),
        ln1c=np.ascontiguousarray(d["L1_attn_ln1_w"].reshape(32, 128).T),
        ln2c=np.ascontiguousarray(d["L1_attn_ln2_w"].reshape(32, 128).T),
        gw1T=np.ascontiguousarray(d["L1_ffn_gw1"].T),
        gw2T=np.ascontiguousarray(d["L1_ffn_gw2"].T),
        gb1r=d["L1_ffn_gb1"].reshape(1, 32).astype(np.float32),
        gb2r=d["L1_ffn_gb2"].reshape(1, 8).astype(np.float32),
        lngr=d["L1_ffn_ln_gamma"].reshape(1, 32).astype(np.float32),
        lnbr=d["L1_ffn_ln_beta"].reshape(1, 32).astype(np.float32),
        ident=np.eye(128, dtype=BF),
        identf=np.eye(128, dtype=np.float32),
    )
    e8 = np.zeros((8, 128), np.float32)
    for e_ in range(8):
        e8[e_, 16 * e_:16 * (e_ + 1)] = 1.0
    shared["e8"] = e8

    maps = []
    for core in range(8):
        b, j = core // 4, core % 4
        pos = core_positions(j)
        x0fm = np.ascontiguousarray(x[b][pos].T).reshape(NC, 128, NT)
        cosr = np.cos(pos.astype(np.float32))
        sinr = np.sin(pos.astype(np.float32))
        maskq = np.zeros((4, 128, 128), np.float32)
        pp = np.arange(128)[:, None]
        qq = np.arange(128)[None, :]
        for s2 in range(4):
            maskq[s2] = np.where((128 * (s2 - j) + pp) <= qq, 0.0, NEG)

        def rrob(qw, qz, sc, nch, nsl):
            chunks = list(range(core, nch, 8))
            qwp = np.zeros((qw.shape[0], nsl * 16), np.int32)
            qzp = np.zeros((qz.shape[0], nsl * 16), np.int32)
            scp = np.zeros((sc.shape[0], nsl * 128), np.float32)
            for s, oc in enumerate(chunks):
                qwp[:, 16 * s:16 * (s + 1)] = qw[:, 16 * oc:16 * (oc + 1)]
                qzp[:, 16 * s:16 * (s + 1)] = qz[:, 16 * oc:16 * (oc + 1)]
                scp[:, 128 * s:128 * (s + 1)] = sc[:, 128 * oc:128 * (oc + 1)]
            return qwp, qzp, scp

        kq, kz, ks = rrob(d["L1_attn_qkv_qw"][:, :512],
                          d["L1_attn_qkv_qz"][:, :512],
                          d["L1_attn_qkv_sc"][:, :4096], 32, 4)
        oq, oz, osc_ = rrob(d["L1_attn_o_qw"], d["L1_attn_o_qz"],
                            d["L1_attn_o_sc"], 32, 4)
        gq, gz, gs = (d["L1_attn_gu_qw"], d["L1_attn_gu_qz"], d["L1_attn_gu_sc"])
        g1, z1, s1 = rrob(gq[:, :1376], gz[:, :1376], gs[:, :11008], 86, 11)
        g2, z2, s2_ = rrob(gq[:, 1376:], gz[:, 1376:], gs[:, 11008:], 86, 11)
        dq_, dz_, ds_ = rrob(d["L1_attn_dn_qw"], d["L1_attn_dn_qz"],
                             d["L1_attn_dn_sc"], 32, 4)

        def gmaj(qw, NGK):
            return np.ascontiguousarray(qw.reshape(NGK, 128, -1))

        m = dict(shared)
        m.update(
            x0fm=x0fm,
            cosb=np.ascontiguousarray(np.broadcast_to(cosr, (128, NT))),
            sinb=np.ascontiguousarray(np.broadcast_to(sinr, (128, NT))),
            cosbq=np.ascontiguousarray(np.broadcast_to(cosr * SCALE, (128, NT))),
            sinbq=np.ascontiguousarray(np.broadcast_to(sinr * SCALE, (128, NT))),
            maskq=maskq,
            qkvqw=gmaj(kq, 32), qkvqz=kz, qkvsc=ks,
            oqw=gmaj(oq, 32), oqz=oz, osc=osc_,
            guqw=gmaj(np.concatenate([g1, g2], 1), 32),
            guqz=np.concatenate([z1, z2], 1),
            gusc=np.concatenate([s1, s2_], 1),
            dnqw=gmaj(dq_, 86), dnqz=dz_, dnsc=ds_,
        )
        maps.append(m)
    return maps


def assemble_output(results):
    out = np.zeros((2, 4096, 4096), np.float32)
    for core in range(8):
        b, j = core // 4, core % 4
        pos = core_positions(j)
        out[b][pos] = results[core]["acc"].reshape(D, NT).T
    return out


def kernel(**inputs):
    maps = build_inputs(inputs)
    nc = build_program(PHASES, taps=DEBUG_TAPS)
    res = run_bass_kernel_spmd(nc, maps, core_ids=list(range(8)))
    return assemble_output(res.results)
